# version 01
# baseline (speedup 1.0000x reference)
import numpy as np
import ml_dtypes
from contextlib import ExitStack

import concourse.bass as bass
import concourse.mybir as mybir
from concourse.bass_utils import run_bass_kernel_spmd

AF = mybir.ActivationFunctionType
ALU = mybir.AluOpType
AX = mybir.AxisListType
F32 = mybir.dt.float32
F16 = mybir.dt.float16
BF16 = mybir.dt.bfloat16
I32 = mybir.dt.int32

D = 2048
NB = D // 128
TOK = 2048
SEQ = 4096
BATCH = 4
NCORES = 8
NH = 16
HD = 128
IFF = 5632
NE = 32
EH = 64
SHIFT = 16.0
EPS_LN = 1e-5
EPS_RMS = 1e-6

ENGS = ("sync", "tensor", "scalar", "vector", "gpsimd")
ESEM = {"sync": "s_sp", "tensor": "s_pe", "scalar": "s_act",
        "vector": "s_dve", "gpsimd": "s_pool"}
NDMASEM = 12
COMPUTE_ENGS = {"tensor"}


class Prog:

    def __init__(self, nc, st):
        self.nc = nc
        self.st = st
        self.ops = {e: [] for e in ENGS}
        semnames = list(ESEM.values())
        for q in ("sync", "gpsimd"):
            for i in range(NDMASEM):
                semnames.append(f"{ESEM[q]}_{i}")
        self.sems = {s: st.enter_context(nc.semaphore(s)) for s in semnames}
        self.cnt = {s: 0 for s in semnames}
        self._dma_rr = {"sync": 0, "gpsimd": 0}
        self.bufs = {}
        self._consts = {}

    def const_col(self, val):
        if val not in self._consts:
            t = self.st.enter_context(
                self.nc.sbuf_tensor(f"cc{len(self._consts)}", [128, 1], F32))
            self.op("vector", lambda e, t=t, v=val: e.memset(t[:], v), w=[t])
            self._consts[val] = t
        return self._consts[val]

    def _bufstate(self, obj):
        k = (id(obj[0]), obj[1]) if isinstance(obj, tuple) else id(obj)
        if k not in self.bufs:
            self.bufs[k] = {"w": None, "r": []}
        return self.bufs[k]

    def op(self, eng, fn, r=(), w=(), extra_waits=(), inc_n=None):
        is_dma = getattr(fn, "_is_dma", False)
        if is_dma:
            rr = self._dma_rr[eng]
            self._dma_rr[eng] = (rr + 1) % NDMASEM
            sem = f"{ESEM[eng]}_{rr}"
            inc_n = 16
        else:
            sem = ESEM[eng]
            if inc_n is None:
                inc_n = 1
        waits = {}

        def add_wait(s, v):
            if v <= 0:
                return
            if s == sem and eng in COMPUTE_ENGS:
                return
            waits[s] = max(waits.get(s, 0), v)

        if is_dma and self.cnt[sem] > 0:
            waits[sem] = self.cnt[sem]
        for obj in r:
            bs = self._bufstate(obj)
            if bs["w"]:
                add_wait(*bs["w"])
        for obj in w:
            bs = self._bufstate(obj)
            if bs["w"]:
                add_wait(*bs["w"])
            for (s, v) in bs["r"]:
                add_wait(s, v)
        for (s, v) in extra_waits:
            add_wait(s, v)

        self.cnt[sem] += inc_n
        post = self.cnt[sem]
        self.ops[eng].append([fn, sorted(waits.items()), inc_n, post, sem])

        for obj in w:
            bs = self._bufstate(obj)
            bs["w"] = (sem, post)
            bs["r"] = []
        for obj in r:
            self._bufstate(obj)["r"].append((sem, post))
        return (sem, post)

    def dma(self, eng, out, in_, r=(), w=(), extra_waits=(), **kw):
        def fn(e, o=out, i=in_, kw=kw):
            return e.dma_start(out=o, in_=i, **kw)
        fn._is_dma = True
        return self.op(eng, fn, r=r, w=w, extra_waits=extra_waits, inc_n=16)

    def barrier_all_stores(self, eng="gpsimd"):
        waits = [(s, self.cnt[s]) for s in self.cnt
                 if self.cnt[s] > 0 and (s.startswith("s_sp") or s.startswith("s_pool"))]
        if waits:
            self.op(eng, None, extra_waits=waits, inc_n=0)

    def emit(self):
        needed = {s: set() for s in self.sems}
        for eng in ENGS:
            for rec in self.ops[eng]:
                for (s, v) in rec[1]:
                    needed[s].add(v)
        renum = {s: {} for s in self.sems}
        sem_ops = {s: [] for s in self.sems}
        for eng in ENGS:
            for rec in self.ops[eng]:
                if rec[2] > 0:
                    sem_ops[rec[4]].append(rec)
        for s, lst in sem_ops.items():
            acc = 0
            dma_sem = s.startswith("s_sp") or s.startswith("s_pool")
            for rec in lst:
                keep = dma_sem or (rec[3] in needed[s])
                if keep:
                    acc += rec[2]
                    renum[s][rec[3]] = acc
                rec.append(keep)
        for eng in ENGS:
            for rec in self.ops[eng]:
                rec[1] = [(s, renum[s][v]) for (s, v) in rec[1]]

        nc = self.nc
        sems = self.sems
        with nc.Block() as block:
            for eng in ENGS:
                lst = self.ops[eng]
                if not lst:
                    continue

                def body(e, lst=lst, eng=eng):
                    hw = {}
                    for rec in lst:
                        fn, waits, inc_n, sem = rec[0], rec[1], rec[2], rec[4]
                        keep = rec[5] if len(rec) > 5 else False
                        for (s, v) in waits:
                            if hw.get(s, 0) >= v:
                                continue
                            hw[s] = v
                            e.wait_ge(sems[s], v)
                        ins = fn(e) if fn is not None else None
                        if inc_n > 0 and keep:
                            if ins is None:
                                e.sem_inc(sems[sem], inc_n)
                            else:
                                ins.then_inc(sems[sem], inc_n)
                getattr(block, eng)(body)


def _bass():
    return bass.Bass("TRN2", target_bir_lowering=False)



def rope_tables_host():
    inv_freq = 1.0 / (10000.0 ** (np.arange(0, HD, 2, dtype=np.float64) / HD))
    t = np.arange(SEQ, dtype=np.float64)
    freqs = np.outer(t, inv_freq)
    emb = np.concatenate([freqs, freqs], axis=-1)
    cos = np.cos(emb).astype(np.float32).T
    sin = np.sin(emb).astype(np.float32).T
    sinf = sin.copy()
    sinf[:64] *= -1.0
    return (np.ascontiguousarray(cos).astype(ml_dtypes.bfloat16),
            np.ascontiguousarray(sinf).astype(ml_dtypes.bfloat16))


def mask_master_host():
    kk = np.arange(128)[:, None]
    m = np.arange(384 + 512 + 384)[None, :]
    return ((m - 384) >= kk).astype(ml_dtypes.bfloat16)



def build_qkv0(debug_ln=False, debug_stats=False):
    nc = _bass()
    x = nc.dram_tensor("x", [TOK, D], F32, kind="ExternalInput").ap()
    wq = nc.dram_tensor("wq", [D, D], F32, kind="ExternalInput").ap()
    wk = nc.dram_tensor("wk", [D, D], F32, kind="ExternalInput").ap()
    wv = nc.dram_tensor("wv", [D, D], F32, kind="ExternalInput").ap()
    bq = nc.dram_tensor("bq", [D], F32, kind="ExternalInput").ap()
    bk = nc.dram_tensor("bk", [D], F32, kind="ExternalInput").ap()
    bv = nc.dram_tensor("bv", [D], F32, kind="ExternalInput").ap()
    cosf = nc.dram_tensor("cosf", [128, TOK], BF16, kind="ExternalInput").ap()
    sinf = nc.dram_tensor("sinf", [128, TOK], BF16, kind="ExternalInput").ap()
    identb = nc.dram_tensor("identb", [128, 128], BF16, kind="ExternalInput").ap()
    identf = nc.dram_tensor("identf", [128, 128], F32, kind="ExternalInput").ap()
    qT = nc.dram_tensor("qT", [D, TOK], BF16, kind="ExternalOutput").ap()
    kT = nc.dram_tensor("kT", [D, TOK], BF16, kind="ExternalOutput").ap()
    vtok = nc.dram_tensor("vtok", [TOK, D], BF16, kind="ExternalOutput").ap()
    xT = nc.dram_tensor("xT", [D, TOK], F32, kind="ExternalOutput").ap()
    lndbg = nc.dram_tensor("lndbg", [D, TOK], BF16, kind="ExternalOutput").ap() if debug_ln else None
    statdbg = nc.dram_tensor("statdbg", [128, 6], F32, kind="ExternalOutput").ap() if debug_stats else None

    with ExitStack() as st:
        p = Prog(nc, st)
        sb = lambda name, shape, dt: st.enter_context(nc.sbuf_tensor(name, shape, dt))
        ps = lambda name, shape, dt: st.enter_context(nc.psum_tensor(name, shape, dt))

        xsb = [sb(f"xsb{i}", [128, D], F32) for i in range(2)]
        scr16 = sb("scr16", [128, D], BF16)
        lntok = [sb(f"lntok{i}", [128, D], BF16) for i in range(2)]
        ln16 = [sb(f"ln16_{i}", [128, TOK], BF16) for i in range(NB)]
        cos_s = sb("cos_s", [128, TOK], BF16)
        sin_s = sb("sin_s", [128, TOK], BF16)
        id_b = sb("id_b", [128, 128], BF16)
        id_f = sb("id_f", [128, 128], F32)
        sxc = sb("sxc", [128, 1], F32)
        sxx = sb("sxx", [128, 1], F32)
        mcol = sb("mcol", [128, 1], F32)
        vcol = sb("vcol", [128, 1], F32)
        rstd = sb("rstd", [128, 1], F32)
        bcol = sb("bcol", [128, 1], F32)
        bias_s = sb("bias_s", [128, 3 * NB], F32)
        ones1f = sb("ones1f", [1, 128], F32)
        bvrow = sb("bvrow", [1, D], F32)
        bvb = sb("bvb", [128, D], F32)
        wcf = sb("wcf", [128, D], F32)
        wcb = [sb(f"wcb{i}", [128, D], BF16) for i in range(2)]
        evac = [sb(f"evac{i}", [128, 512], BF16) for i in range(2)]
        rot = [sb(f"rot{i}", [128, 512], BF16) for i in range(2)]
        oux = [sb(f"oux{i}", [128, 512], BF16) for i in range(2)]
        wvtile = [sb(f"wvt{i}", [128, 512], BF16) for i in range(NB)]
        wvf = sb("wvf", [128, 512], F32)
        xTev = [sb(f"xTev{i}", [128, 128], F32) for i in range(2)]

        ps_t = [ps(f"ps_t{i}", [128, 128], BF16) for i in range(2)]
        ps_tf = [ps(f"ps_tf{i}", [128, 128], F32) for i in range(2)]
        ps_mm = [ps(f"ps_mm{i}", [128, 512], F32) for i in range(2)]
        ps_v = [ps(f"ps_v{i}", [128, 512], F32) for i in range(2)]

        p.dma("sync", cos_s[:], cosf, w=[cos_s])
        p.dma("sync", sin_s[:], sinf, w=[sin_s])
        p.dma("sync", id_b[:], identb, w=[id_b])
        p.dma("sync", id_f[:], identf, w=[id_f])
        for i, b in enumerate((bq, bk, bv)):
            for ob in range(NB):
                p.dma("sync", bias_s[:, i * NB + ob: i * NB + ob + 1],
                      b[ob * 128:(ob + 1) * 128].rearrange("(p one) -> p one", one=1),
                      w=[bias_s])
        p.dma("sync", bvrow[0:1, :], bv.rearrange("(one d) -> one d", one=1), w=[bvrow])
        p.op("vector", lambda e: e.memset(ones1f[:], 1.0), w=[ones1f])
        for et in range(4):
            pp = et % 2
            p.op("tensor", lambda e, et=et, pp=pp: e.matmul(
                out=ps_v[pp][:], lhsT=ones1f[:], rhs=bvrow[0:1, et * 512:(et + 1) * 512],
                start=True, stop=True), r=[ones1f, bvrow], w=[ps_v[pp]])
            p.op("vector", lambda e, et=et, pp=pp: e.tensor_copy(
                out=bvb[:, et * 512:(et + 1) * 512], in_=ps_v[pp][:]),
                r=[ps_v[pp]], w=[bvb])

        for tb in range(NB):
            bx = tb % 2
            p.dma("sync", xsb[bx][:], x[tb * 128:(tb + 1) * 128, :], w=[xsb[bx]])
            p.op("scalar", lambda e, bx=bx: e.activation(
                out=scr16[:], in_=xsb[bx][:], func=AF.Copy, accum_out=sxc[:]),
                r=[xsb[bx]], w=[scr16, sxc])
            p.op("scalar", lambda e, bx=bx: e.activation(
                out=scr16[:], in_=xsb[bx][:], func=AF.Square, accum_out=sxx[:]),
                r=[xsb[bx]], w=[scr16, sxx])
            p.op("vector", lambda e: e.tensor_scalar(
                out=mcol[:], in0=sxc[:], scalar1=1.0 / D, scalar2=None, op0=ALU.mult),
                r=[sxc], w=[mcol])
            p.op("vector", lambda e: e.tensor_scalar(
                out=vcol[:], in0=sxx[:], scalar1=1.0 / D, scalar2=None, op0=ALU.mult),
                r=[sxx], w=[vcol])
            p.op("vector", lambda e: e.scalar_tensor_tensor(
                out=rstd[:], in0=mcol[:], scalar=-1.0, in1=mcol[:],
                op0=ALU.mult, op1=ALU.mult), r=[mcol], w=[rstd])
            p.op("vector", lambda e: e.tensor_add(out=vcol[:], in0=vcol[:], in1=rstd[:]),
                r=[vcol, rstd], w=[vcol])
            epsc = p.const_col(EPS_LN)
            p.op("scalar", lambda e, epsc=epsc: e.activation(
                out=vcol[:], in_=vcol[:], func=AF.Sqrt, bias=epsc[:]),
                r=[vcol, epsc], w=[vcol])
            p.op("vector", lambda e: e.reciprocal(out=rstd[:], in_=vcol[:]),
                r=[vcol], w=[rstd])
            p.op("vector", lambda e: e.scalar_tensor_tensor(
                out=bcol[:], in0=mcol[:], scalar=-1.0, in1=rstd[:],
                op0=ALU.mult, op1=ALU.mult), r=[mcol, rstd], w=[bcol])
            if debug_stats and tb == 0:
                for i, col in enumerate((sxc, sxx, mcol, vcol, rstd, bcol)):
                    p.dma("gpsimd", statdbg[:, i:i + 1], col[:], r=[col],
                          allow_slow_non_contiguous=True)
            lt = lntok[bx]
            p.op("scalar", lambda e, bx=bx, lt=lt: e.activation(
                out=lt[:], in_=xsb[bx][:], func=AF.Identity, scale=rstd[:], bias=bcol[:]),
                r=[xsb[bx], rstd, bcol], w=[lt])
            for fb in range(NB):
                tp = fb % 2
                p.op("tensor", lambda e, fb=fb, tp=tp, lt=lt: e.transpose(
                    out=ps_t[tp][:], in_=lt[:, fb * 128:(fb + 1) * 128],
                    identity=id_b[:]), r=[lt, id_b], w=[ps_t[tp]])
                p.op("vector", lambda e, fb=fb, tp=tp, tb=tb: e.tensor_copy(
                    out=ln16[fb][:, tb * 128:(tb + 1) * 128], in_=ps_t[tp][:]),
                    r=[ps_t[tp]], w=[ln16[fb]])
                p.op("tensor", lambda e, fb=fb, tp=tp, bx=bx: e.transpose(
                    out=ps_tf[tp][:], in_=xsb[bx][:, fb * 128:(fb + 1) * 128],
                    identity=id_f[:]), r=[xsb[bx], id_f], w=[ps_tf[tp]])
                p.op("vector", lambda e, tp=tp: e.tensor_copy(
                    out=xTev[tp][:], in_=ps_tf[tp][:]), r=[ps_tf[tp]], w=[xTev[tp]])
                p.dma("gpsimd", xT[fb * 128:(fb + 1) * 128, tb * 128:(tb + 1) * 128],
                      xTev[tp][:], r=[xTev[tp]])

        def proj_fm(wsrc, bias_idx, outT, do_rope):
            for ob in range(NB):
                wb = ob % 2
                for fb in range(NB):
                    p.dma("sync", wcf[:, fb * 128:(fb + 1) * 128],
                          wsrc[fb * 128:(fb + 1) * 128, ob * 128:(ob + 1) * 128],
                          w=[wcf])
                p.op("vector", lambda e, wb=wb: e.tensor_copy(
                    out=wcb[wb][:], in_=wcf[:]), r=[wcf], w=[wcb[wb]])
                for tb in range(4):
                    pp = tb % 2
                    eb = tb % 2
                    for fb in range(NB):
                        p.op("tensor", lambda e, fb=fb, tb=tb, pp=pp, wb=wb: e.matmul(
                            out=ps_mm[pp][:], lhsT=wcb[wb][:, fb * 128:(fb + 1) * 128],
                            rhs=ln16[fb][:, tb * 512:(tb + 1) * 512],
                            start=(fb == 0), stop=(fb == NB - 1)),
                            r=[wcb[wb], ln16[fb]], w=[ps_mm[pp]])
                    p.op("scalar", lambda e, pp=pp, eb=eb, ob=ob: e.activation(
                        out=evac[eb][:], in_=ps_mm[pp][:], func=AF.Identity,
                        bias=bias_s[:, bias_idx * NB + ob: bias_idx * NB + ob + 1]),
                        r=[ps_mm[pp], bias_s], w=[evac[eb]])
                    if do_rope:
                        sl = slice(tb * 512, tb * 512 + 512)
                        p.op("vector", lambda e, eb=eb: e.tensor_copy(
                            out=rot[eb][0:64, :], in_=evac[eb][64:128, :]),
                            r=[evac[eb]], w=[rot[eb]])
                        p.op("vector", lambda e, eb=eb: e.tensor_copy(
                            out=rot[eb][64:128, :], in_=evac[eb][0:64, :]),
                            r=[evac[eb]], w=[rot[eb]])
                        p.op("vector", lambda e, eb=eb, sl=sl: e.tensor_mul(
                            out=oux[eb][:], in0=evac[eb][:], in1=cos_s[:, sl]),
                            r=[evac[eb], cos_s], w=[oux[eb]])
                        p.op("vector", lambda e, eb=eb, sl=sl: e.tensor_mul(
                            out=rot[eb][:], in0=rot[eb][:], in1=sin_s[:, sl]),
                            r=[rot[eb], sin_s], w=[rot[eb]])
                        p.op("vector", lambda e, eb=eb: e.tensor_add(
                            out=oux[eb][:], in0=oux[eb][:], in1=rot[eb][:]),
                            r=[oux[eb], rot[eb]], w=[oux[eb]])
                        src = oux[eb]
                    else:
                        src = evac[eb]
                    p.dma("gpsimd", outT[ob * 128:(ob + 1) * 128, tb * 512:(tb + 1) * 512],
                          src[:], r=[src])

        if debug_ln:
            for fb in range(NB):
                p.dma("gpsimd", lndbg[fb * 128:(fb + 1) * 128, :], ln16[fb][:], r=[ln16[fb]])
        proj_fm(wq, 0, qT, True)
        proj_fm(wk, 1, kT, True)

        for et in range(4):
            for fb in range(NB):
                p.dma("sync", wvf[:],
                      wv[fb * 128:(fb + 1) * 128, et * 512:(et + 1) * 512], w=[wvf])
                p.op("vector", lambda e, fb=fb: e.tensor_copy(
                    out=wvtile[fb][:], in_=wvf[:]), r=[wvf], w=[wvtile[fb]])
            for tb in range(NB):
                pp = tb % 2
                eb = tb % 2
                for fb in range(NB):
                    p.op("tensor", lambda e, fb=fb, tb=tb, pp=pp: e.matmul(
                        out=ps_v[pp][:], lhsT=ln16[fb][:, tb * 128:(tb + 1) * 128],
                        rhs=wvtile[fb][:], start=(fb == 0), stop=(fb == NB - 1)),
                        r=[ln16[fb], wvtile[fb]], w=[ps_v[pp]])
                p.op("vector", lambda e, pp=pp, eb=eb, et=et: e.tensor_add(
                    out=oux[eb][:], in0=ps_v[pp][:], in1=bvb[:, et * 512:(et + 1) * 512]),
                    r=[ps_v[pp], bvb], w=[oux[eb]])
                p.dma("gpsimd", vtok[tb * 128:(tb + 1) * 128, et * 512:(et + 1) * 512],
                      oux[eb][:], r=[oux[eb]])

        p.barrier_all_stores()
        p.emit()
    return nc


def build_attn0(q0_glob):
    klen = q0_glob + TOK
    nkc = klen // 128
    nc = _bass()
    qTh = nc.dram_tensor("qTh", [1024, TOK], BF16, kind="ExternalInput").ap()
    kTh = nc.dram_tensor("kTh", [1024, klen], BF16, kind="ExternalInput").ap()
    vh = nc.dram_tensor("vh", [klen, 1024], BF16, kind="ExternalInput").ap()
    maskm = nc.dram_tensor("maskm", [128, 1280], BF16, kind="ExternalInput").ap()
    oTh = nc.dram_tensor("oTh", [1024, TOK], BF16, kind="ExternalOutput").ap()

    with ExitStack() as st:
        p = Prog(nc, st)
        sb = lambda name, shape, dt: st.enter_context(nc.sbuf_tensor(name, shape, dt))
        ps = lambda name, shape, dt: st.enter_context(nc.psum_tensor(name, shape, dt))

        k_s = [sb(f"k_s{i}", [128, klen], BF16) for i in range(2)]
        v_s = [sb(f"v_s{i}", [128, nkc * 128], BF16) for i in range(2)]
        q_s = [sb(f"q_s{i}", [128, TOK], BF16) for i in range(2)]
        mask_s = sb("mask_s", [128, 1280], BF16)
        ones_b = sb("ones_b", [128, 1], BF16)
        ones1f = sb("ones1f", [1, 128], F32)
        pt = [sb(f"pt{i}", [128, 512], BF16) for i in range(3)]
        l_sb = sb("l_sb", [1, 512], F32)
        rinv = sb("rinv", [1, 512], F32)
        rb_sb = sb("rb_sb", [128, 512], F32)
        o16 = [sb(f"o16{i}", [128, 512], BF16) for i in range(2)]
        sc_col = p.const_col(1.0 / np.sqrt(HD))
        sh_col = p.const_col(-SHIFT)

        ps_s = [ps(f"ps_s{i}", [128, 512], F32) for i in range(2)]
        ps_o = ps("ps_o", [128, 512], F32)
        ps_l = ps("ps_l", [1, 512], F32)
        ps_rb = ps("ps_rb", [128, 512], F32)

        p.dma("sync", mask_s[:], maskm, w=[mask_s])
        p.op("vector", lambda e: e.memset(ones_b[:], 1.0), w=[ones_b])
        p.op("vector", lambda e: e.memset(ones1f[:], 1.0), w=[ones1f])

        for h in range(8):
            hb = h % 2
            p.dma("sync", k_s[hb][:], kTh[h * 128:(h + 1) * 128, :], w=[k_s[hb]])
            for kc in range(nkc):
                p.dma("sync", v_s[hb][:, kc * 128:(kc + 1) * 128],
                      vh[kc * 128:(kc + 1) * 128, h * 128:(h + 1) * 128],
                      w=[v_s[hb]] if kc == 0 else (), r=[v_s[hb]] if kc > 0 else ())
            p.dma("sync", q_s[hb][:], qTh[h * 128:(h + 1) * 128, :], w=[q_s[hb]])
            for qt in range(4):
                kcmax = (q0_glob + qt * 512) // 128 + 4
                for kc in range(kcmax):
                    pp = kc % 2
                    pb = kc % 3
                    p.op("tensor", lambda e, pp=pp, hb=hb, kc=kc, qt=qt: e.matmul(
                        out=ps_s[pp][:], lhsT=k_s[hb][:, kc * 128:(kc + 1) * 128],
                        rhs=q_s[hb][:, qt * 512:(qt + 1) * 512],
                        start=True, stop=True),
                        r=[k_s[hb], q_s[hb]], w=[ps_s[pp]])
                    p.op("scalar", lambda e, pp=pp, pb=pb: e.activation(
                        out=pt[pb][:], in_=ps_s[pp][:], func=AF.Exp,
                        scale=sc_col[:], bias=sh_col[:]),
                        r=[ps_s[pp], sc_col, sh_col], w=[pt[pb]])
                    if kc >= kcmax - 4:
                        dd = 384 - 128 * (kc - (kcmax - 4))
                        p.op("vector", lambda e, pb=pb, dd=dd: e.tensor_mul(
                            out=pt[pb][:], in0=pt[pb][:], in1=mask_s[:, dd:dd + 512]),
                            r=[pt[pb], mask_s], w=[pt[pb]])
                    p.op("tensor", lambda e, pb=pb, hb=hb, kc=kc, last=(kc == kcmax - 1): e.matmul(
                        out=ps_o[:], lhsT=v_s[hb][:, kc * 128:(kc + 1) * 128],
                        rhs=pt[pb][:], start=(kc == 0), stop=last),
                        r=[v_s[hb], pt[pb]], w=[ps_o])
                    p.op("tensor", lambda e, pb=pb, kc=kc, last=(kc == kcmax - 1): e.matmul(
                        out=ps_l[:], lhsT=ones_b[:], rhs=pt[pb][:],
                        start=(kc == 0), stop=last),
                        r=[ones_b, pt[pb]], w=[ps_l])
                p.op("vector", lambda e: e.tensor_copy(out=l_sb[:], in_=ps_l[:]),
                     r=[ps_l], w=[l_sb])
                p.op("vector", lambda e: e.reciprocal(out=rinv[:], in_=l_sb[:]),
                     r=[l_sb], w=[rinv])
                p.op("tensor", lambda e: e.matmul(
                    out=ps_rb[:], lhsT=ones1f[:], rhs=rinv[:], start=True, stop=True),
                    r=[ones1f, rinv], w=[ps_rb])
                p.op("vector", lambda e: e.tensor_copy(out=rb_sb[:], in_=ps_rb[:]),
                     r=[ps_rb], w=[rb_sb])
                ob = qt % 2
                p.op("vector", lambda e, ob=ob: e.tensor_mul(
                    out=o16[ob][:], in0=ps_o[:], in1=rb_sb[:]),
                    r=[ps_o, rb_sb], w=[o16[ob]])
                p.dma("gpsimd", oTh[h * 128:(h + 1) * 128, qt * 512:(qt + 1) * 512],
                      o16[ob][:], r=[o16[ob]])

        p.barrier_all_stores()
        p.emit()
    return nc


def _dequant_piece(p, sb, pw, tmpa, tmpb, wsl, src_pT, scales, wout, nrow_w, ncol,
                   wc, cnt, tag):
    nc = p.nc
    p.dma("sync", pw[0:cnt, :], src_pT[wc * 128: wc * 128 + cnt, :], w=[pw])
    sct = sb
    for g in range(8):
        r0 = g * 16
        if r0 >= cnt:
            break
        rcnt = min(16, cnt - r0)
        p.dma("sync", sct[r0:r0 + rcnt, :],
              scales[8 * wc + g: 8 * wc + g + 1, :].broadcast_to([rcnt, ncol]),
              w=[(sct, g)])
    for k in range(8):
        if k < 3:
            sa = lambda e, k=k, cnt=cnt: e.tensor_scalar(
                out=tmpa[0:cnt, :], in0=pw[0:cnt, :], scalar1=12 - 4 * k,
                scalar2=0x8000, op0=ALU.logical_shift_left, op1=ALU.bitwise_and)
        elif k == 3:
            sa = lambda e, cnt=cnt: e.tensor_scalar(
                out=tmpa[0:cnt, :], in0=pw[0:cnt, :], scalar1=0x8000,
                scalar2=None, op0=ALU.bitwise_and)
        else:
            sa = lambda e, k=k, cnt=cnt: e.tensor_scalar(
                out=tmpa[0:cnt, :], in0=pw[0:cnt, :], scalar1=4 * k - 12,
                scalar2=0x8000, op0=ALU.logical_shift_right, op1=ALU.bitwise_and)
        p.op("vector", sa, r=[pw], w=[tmpa])
        if k < 3:
            sm = lambda e, k=k, cnt=cnt: e.tensor_scalar(
                out=tmpb[0:cnt, :], in0=pw[0:cnt, :], scalar1=9 - 4 * k,
                scalar2=0x0E00, op0=ALU.logical_shift_left, op1=ALU.bitwise_and)
        else:
            sm = lambda e, k=k, cnt=cnt: e.tensor_scalar(
                out=tmpb[0:cnt, :], in0=pw[0:cnt, :], scalar1=4 * k - 9,
                scalar2=0x0E00, op0=ALU.logical_shift_right, op1=ALU.bitwise_and)
        p.op("vector", sm, r=[pw], w=[tmpb])
        p.op("vector", lambda e, cnt=cnt: e.tensor_tensor(
            out=tmpb[0:cnt, :], in0=tmpa[0:cnt, :], in1=tmpb[0:cnt, :],
            op=ALU.bitwise_or), r=[tmpa, tmpb], w=[tmpb])
        fpv = tmpb[0:cnt, :].bitcast(F16)[:, ::2]
        p.op("vector", lambda e, fpv=fpv, cnt=cnt: e.scalar_tensor_tensor(
            out=wsl[0:cnt, :], in0=fpv, scalar=16384.0, in1=sct[0:cnt, :],
            op0=ALU.mult, op1=ALU.mult),
            r=[tmpb] + [(sct, g) for g in range(8)], w=[wsl])
        wview = wout.rearrange("(w k) f -> k w f", k=8)[k, wc * 128: wc * 128 + cnt, :]
        p.dma("gpsimd", wview, wsl[0:cnt, :], r=[wsl])


def build_dequant():
    nc = _bass()
    gup = nc.dram_tensor("gup", [256, 1408], I32, kind="ExternalInput").ap()
    gus = nc.dram_tensor("gus", [16, 1408], F16, kind="ExternalInput").ap()
    dnp = nc.dram_tensor("dnp", [704, 256], I32, kind="ExternalInput").ap()
    dns = nc.dram_tensor("dns", [44, 256], F16, kind="ExternalInput").ap()
    wgu = nc.dram_tensor("wgu", [2048, 1408], BF16, kind="ExternalOutput").ap()
    wdn = nc.dram_tensor("wdn", [5632, 256], BF16, kind="ExternalOutput").ap()

    with ExitStack() as st:
        p = Prog(nc, st)
        sbf = lambda name, shape, dt: st.enter_context(nc.sbuf_tensor(name, shape, dt))
        pw = sbf("pw", [128, 1408], I32)
        sct = sbf("sct", [128, 1408], F16)
        tmpa = sbf("tmpa", [128, 1408], I32)
        tmpb = sbf("tmpb", [128, 1408], I32)
        wsl = sbf("wsl", [128, 1408], BF16)
        for wc in range(2):
            _dequant_piece(p, sct, pw, tmpa, tmpb, wsl, gup, gus, wgu,
                           2048, 1408, wc, 128, "gu")
        pw2 = sbf("pw2", [128, 256], I32)
        sct2 = sbf("sct2", [128, 256], F16)
        tmpa2 = sbf("tmpa2", [128, 256], I32)
        tmpb2 = sbf("tmpb2", [128, 256], I32)
        wsl2 = sbf("wsl2", [128, 256], BF16)
        for wc in range(6):
            cnt = 128 if wc < 5 else 64
            _dequant_piece(p, sct2, pw2, tmpa2, tmpb2, wsl2, dnp, dns, wdn,
                           5632, 256, wc, cnt, "dn")
        p.barrier_all_stores()
        p.emit()
    return nc


def _ln_scratch(p, sb):
    sc = {}
    sc["x16"] = [sb("lnsc_x16_%d" % i, [128, 512], BF16) for i in range(2)]
    sc["xsq"] = [sb("lnsc_xsq_%d" % i, [128, 512], BF16) for i in range(2)]
    sc["ones_b"] = sb("lnsc_ones", [128, 1], BF16)
    sc["ones1f"] = sb("lnsc_ones1", [1, 128], F32)
    sc["stA"] = sb("lnsc_stA", [1, 512], F32)
    sc["stB"] = sb("lnsc_stB", [1, 512], F32)
    sc["stC"] = sb("lnsc_stC", [1, 512], F32)
    sc["abc"] = sb("lnsc_abc", [128, 512], F32)
    sc["bbc"] = sb("lnsc_bbc", [128, 512], F32)
    p.op("vector", lambda e: e.memset(sc["ones_b"][:], 1.0), w=[sc["ones_b"]])
    p.op("vector", lambda e: e.memset(sc["ones1f"][:], 1.0), w=[sc["ones1f"]])
    return sc


def emit_ln_fm(p, sc, ps_m, ps_mo, ps_s, ps_so, ps_b, ps_bo, xs, lns):
    def unpack(z):
        if isinstance(z, tuple) and len(z) == 2 and not isinstance(z[1], int):
            return z
        if isinstance(z, tuple):
            return z
        return (z[:], z)
    xs = [unpack(z) for z in xs]
    lns = [unpack(z) for z in lns]
    x16, xsq = sc["x16"], sc["xsq"]
    ones_b, ones1f = sc["ones_b"], sc["ones1f"]
    stA, stB, stC, abc, bbc = sc["stA"], sc["stB"], sc["stC"], sc["abc"], sc["bbc"]
    epsc = p.const_col(EPS_LN)
    for fb in range(NB):
        pp = fb % 2
        p.op("scalar", lambda e, fb=fb, pp=pp: e.activation(
            out=x16[pp][:], in_=xs[fb][0], func=AF.Copy),
            r=[xs[fb][1]], w=[x16[pp]])
        p.op("tensor", lambda e, fb=fb, pp=pp: e.matmul(
            out=ps_m, lhsT=ones_b[:], rhs=x16[pp][:],
            start=(fb == 0), stop=(fb == NB - 1)),
            r=[ones_b, x16[pp]], w=[ps_mo])
        p.op("scalar", lambda e, fb=fb, pp=pp: e.activation(
            out=xsq[pp][:], in_=xs[fb][0], func=AF.Square),
            r=[xs[fb][1]], w=[xsq[pp]])
        p.op("tensor", lambda e, fb=fb, pp=pp: e.matmul(
            out=ps_s, lhsT=ones_b[:], rhs=xsq[pp][:],
            start=(fb == 0), stop=(fb == NB - 1)),
            r=[ones_b, xsq[pp]], w=[ps_so])
    p.op("vector", lambda e: e.tensor_scalar(
        out=stA[:], in0=ps_m, scalar1=1.0 / D, scalar2=None, op0=ALU.mult),
        r=[ps_mo], w=[stA])
    p.op("vector", lambda e: e.tensor_scalar(
        out=stB[:], in0=ps_s, scalar1=1.0 / D, scalar2=None, op0=ALU.mult),
        r=[ps_so], w=[stB])
    p.op("vector", lambda e: e.scalar_tensor_tensor(
        out=stC[:], in0=stA[:], scalar=-1.0, in1=stA[:],
        op0=ALU.mult, op1=ALU.mult), r=[stA], w=[stC])
    p.op("vector", lambda e: e.tensor_add(out=stB[:], in0=stB[:], in1=stC[:]),
         r=[stB, stC], w=[stB])
    p.op("scalar", lambda e, epsc=epsc: e.activation(
        out=stB[:], in_=stB[:], func=AF.Sqrt, bias=epsc[0:1, :]),
        r=[stB, epsc], w=[stB])
    p.op("vector", lambda e: e.reciprocal(out=stB[:], in_=stB[:]),
         r=[stB], w=[stB])
    p.op("vector", lambda e: e.scalar_tensor_tensor(
        out=stC[:], in0=stA[:], scalar=-1.0, in1=stB[:],
        op0=ALU.mult, op1=ALU.mult), r=[stA, stB], w=[stC])
    p.op("tensor", lambda e: e.matmul(
        out=ps_b, lhsT=ones1f[:], rhs=stB[:], start=True, stop=True),
        r=[ones1f, stB], w=[ps_bo])
    p.op("vector", lambda e: e.tensor_copy(out=abc[:], in_=ps_b),
         r=[ps_bo], w=[abc])
    p.op("tensor", lambda e: e.matmul(
        out=ps_b, lhsT=ones1f[:], rhs=stC[:], start=True, stop=True),
        r=[ones1f, stC], w=[ps_bo])
    p.op("vector", lambda e: e.tensor_copy(out=bbc[:], in_=ps_b),
         r=[ps_bo], w=[bbc])
    for fb in range(NB):
        p.op("vector", lambda e, fb=fb: e.tensor_mul(
            out=lns[fb][0], in0=xs[fb][0], in1=abc[:]),
            r=[xs[fb][1], abc], w=[lns[fb][1]])
        p.op("vector", lambda e, fb=fb: e.tensor_add(
            out=lns[fb][0], in0=lns[fb][0], in1=bbc[:]),
            r=[lns[fb][1], bbc], w=[lns[fb][1]])


def build_wo_ffn0():
    nc = _bass()
    xT = nc.dram_tensor("xT", [D, TOK], F32, kind="ExternalInput").ap()
    oT = nc.dram_tensor("oT", [D, TOK], BF16, kind="ExternalInput").ap()
    woT = nc.dram_tensor("woT", [D, D], F32, kind="ExternalInput").ap()
    bo = nc.dram_tensor("bo", [D], F32, kind="ExternalInput").ap()
    wgu5 = nc.dram_tensor("wgu5", [88, 128, NB, 128], BF16, kind="ExternalInput").ap()
    wdn5 = nc.dram_tensor("wdn5", [NB, 128, 44, 128], BF16, kind="ExternalInput").ap()
    x2T = nc.dram_tensor("x2T", [D, TOK], F32, kind="ExternalOutput").ap()

    with ExitStack() as st:
        p = Prog(nc, st)
        sb = lambda name, shape, dt: st.enter_context(nc.sbuf_tensor(name, shape, dt))
        ps = lambda name, shape, dt: st.enter_context(nc.psum_tensor(name, shape, dt))

        oTs = [sb(f"oTs_{i}", [128, 512], BF16) for i in range(NB)]
        wcf = sb("wcf", [128, D], F32)
        wcb = [sb(f"wcb{i}", [128, D], BF16) for i in range(2)]
        bo_s = sb("bo_s", [128, NB], F32)
        x1s = [sb(f"x1s_{i}", [128, 512], F32) for i in range(NB)]
        lns = [sb(f"lns_{i}", [128, 512], BF16) for i in range(NB)]
        hj = [sb(f"hj_{i}", [128, 512], BF16) for i in range(44)]
        y16 = [sb(f"y16_{i}", [128, 512], BF16) for i in range(NB)]
        sg = sb("sg", [128, 512], BF16)
        ysq = [sb(f"ysq{i}", [128, 512], BF16) for i in range(2)]
        wgj = [sb(f"wgj{i}", [128, D], BF16) for i in range(2)]
        wuj = [sb(f"wuj{i}", [128, D], BF16) for i in range(2)]
        wdnb = [sb(f"wdnb{i}", [128, 44 * 128], BF16) for i in range(2)]
        rcol = sb("rcol", [1, 512], F32)
        rb = sb("rb", [128, 512], F32)
        tmpb = sb("tmpb", [128, 512], BF16)
        ones_b2 = sb("ones_b2", [128, 1], BF16)
        ones1f2 = sb("ones1f2", [1, 128], F32)
        ps_a = [ps(f"ps_a{i}", [128, 512], F32) for i in range(2)]
        ps_g = ps("ps_g", [128, 512], F32)
        ps_u = ps("ps_u", [128, 512], F32)
        ps_y = [ps(f"ps_y{i}", [128, 512], F32) for i in range(2)]
        ps_rb = ps("ps_rb", [128, 512], F32)
        epsr = p.const_col(EPS_RMS)
        p.op("vector", lambda e: e.memset(ones_b2[:], 1.0), w=[ones_b2])
        p.op("vector", lambda e: e.memset(ones1f2[:], 1.0), w=[ones1f2])
        lnsc = _ln_scratch(p, sb)

        for ob in range(NB):
            p.dma("sync", bo_s[:, ob:ob + 1],
                  bo[ob * 128:(ob + 1) * 128].rearrange("(p one) -> p one", one=1),
                  w=[(bo_s, ob)])

        for tt in range(4):
            sl = slice(tt * 512, tt * 512 + 512)
            for fb in range(NB):
                p.dma("sync", oTs[fb][:], oT[fb * 128:(fb + 1) * 128, sl],
                      w=[oTs[fb]])
            for ob in range(NB):
                wb = ob % 2
                pp = ob % 2
                for fb in range(NB):
                    p.dma("sync", wcf[:, fb * 128:(fb + 1) * 128],
                          woT[fb * 128:(fb + 1) * 128, ob * 128:(ob + 1) * 128],
                          w=[(wcf, fb)])
                p.op("vector", lambda e, wb=wb: e.tensor_copy(
                    out=wcb[wb][:], in_=wcf[:]),
                    r=[(wcf, fb2) for fb2 in range(NB)], w=[wcb[wb]])
                p.dma("sync", x1s[ob][:], xT[ob * 128:(ob + 1) * 128, sl],
                      w=[x1s[ob]])
                for fb in range(NB):
                    p.op("tensor", lambda e, fb=fb, pp=pp, wb=wb: e.matmul(
                        out=ps_a[pp][:], lhsT=wcb[wb][:, fb * 128:(fb + 1) * 128],
                        rhs=oTs[fb][:], start=(fb == 0), stop=(fb == NB - 1)),
                        r=[wcb[wb], oTs[fb]], w=[ps_a[pp]])
                p.op("vector", lambda e, pp=pp, ob=ob: e.scalar_tensor_tensor(
                    out=x1s[ob][:], in0=ps_a[pp][:], scalar=bo_s[:, ob:ob + 1],
                    in1=x1s[ob][:], op0=ALU.add, op1=ALU.add),
                    r=[ps_a[pp], (bo_s, ob), x1s[ob]], w=[x1s[ob]])
            emit_ln_fm(p, lnsc, ps_g[0:1, :], ps_g, ps_u[0:1, :], ps_u,
                       ps_rb[:], ps_rb, x1s, lns)
            for j in range(44):
                jb = j % 2
                p.dma("sync", wgj[jb][:],
                      wgu5[j].rearrange("p fb c -> p (fb c)"), w=[wgj[jb]])
                p.dma("sync", wuj[jb][:],
                      wgu5[44 + j].rearrange("p fb c -> p (fb c)"), w=[wuj[jb]])
                for fb in range(NB):
                    p.op("tensor", lambda e, fb=fb, jb=jb: e.matmul(
                        out=ps_g[:], lhsT=wgj[jb][:, fb * 128:(fb + 1) * 128],
                        rhs=lns[fb][:], start=(fb == 0), stop=(fb == NB - 1)),
                        r=[wgj[jb], lns[fb]], w=[ps_g])
                    p.op("tensor", lambda e, fb=fb, jb=jb: e.matmul(
                        out=ps_u[:], lhsT=wuj[jb][:, fb * 128:(fb + 1) * 128],
                        rhs=lns[fb][:], start=(fb == 0), stop=(fb == NB - 1)),
                        r=[wuj[jb], lns[fb]], w=[ps_u])
                p.op("scalar", lambda e: e.activation(
                    out=sg[:], in_=ps_g[:], func=AF.Silu), r=[ps_g], w=[sg])
                p.op("vector", lambda e, j=j: e.tensor_mul(
                    out=hj[j][:], in0=sg[:], in1=ps_u[:]), r=[sg, ps_u], w=[hj[j]])
            for db in range(NB):
                pp = db % 2
                p.dma("sync", wdnb[pp][:],
                      wdn5[db].rearrange("p fb c -> p (fb c)"), w=[wdnb[pp]])
                for j in range(44):
                    p.op("tensor", lambda e, j=j, pp=pp: e.matmul(
                        out=ps_y[pp][:], lhsT=wdnb[pp][:, j * 128:(j + 1) * 128],
                        rhs=hj[j][:], start=(j == 0), stop=(j == 43)),
                        r=[wdnb[pp], hj[j]], w=[ps_y[pp]])
                p.op("scalar", lambda e, db=db, pp=pp: e.activation(
                    out=y16[db][:], in_=ps_y[pp][:], func=AF.Copy),
                    r=[ps_y[pp]], w=[y16[db]])
                p.op("scalar", lambda e, db=db, pp=pp: e.activation(
                    out=ysq[pp][:], in_=y16[db][:], func=AF.Square),
                    r=[y16[db]], w=[ysq[pp]])
                p.op("tensor", lambda e, db=db, pp=pp: e.matmul(
                    out=ps_rb[0:1, :], lhsT=ones_b2[:], rhs=ysq[pp][:],
                    start=(db == 0), stop=(db == NB - 1)),
                    r=[ones_b2, ysq[pp]], w=[ps_rb])
            p.op("vector", lambda e: e.tensor_scalar(
                out=rcol[:], in0=ps_rb[0:1, :], scalar1=1.0 / D, scalar2=None,
                op0=ALU.mult), r=[ps_rb], w=[rcol])
            p.op("scalar", lambda e: e.activation(
                out=rcol[:], in_=rcol[:], func=AF.Sqrt, bias=epsr[0:1, :]),
                r=[rcol, epsr], w=[rcol])
            p.op("vector", lambda e: e.reciprocal(out=rcol[:], in_=rcol[:]),
                 r=[rcol], w=[rcol])
            p.op("tensor", lambda e: e.matmul(
                out=ps_rb[:], lhsT=ones1f2[:], rhs=rcol[:], start=True, stop=True),
                r=[ones1f2, rcol], w=[ps_rb])
            p.op("vector", lambda e: e.tensor_copy(out=rb[:], in_=ps_rb[:]),
                 r=[ps_rb], w=[rb])
            for db in range(NB):
                p.op("vector", lambda e, db=db: e.tensor_mul(
                    out=tmpb[:], in0=y16[db][:], in1=rb[:]),
                    r=[y16[db], rb], w=[tmpb])
                p.op("vector", lambda e, db=db: e.tensor_add(
                    out=x1s[db][:], in0=x1s[db][:], in1=tmpb[:]),
                    r=[x1s[db], tmpb], w=[x1s[db]])
                p.dma("gpsimd", x2T[db * 128:(db + 1) * 128, sl], x1s[db][:],
                      r=[x1s[db]])

        p.barrier_all_stores()
        p.emit()
    return nc


def build_qkv1():
    nc = _bass()
    x2T = nc.dram_tensor("x2T", [D, TOK], F32, kind="ExternalInput").ap()
    wq = nc.dram_tensor("wq", [D, D], F32, kind="ExternalInput").ap()
    wk = nc.dram_tensor("wk", [D, D], F32, kind="ExternalInput").ap()
    wv = nc.dram_tensor("wv", [D, D], F32, kind="ExternalInput").ap()
    qT = nc.dram_tensor("qT", [D, TOK], BF16, kind="ExternalOutput").ap()
    kT = nc.dram_tensor("kT", [D, TOK], BF16, kind="ExternalOutput").ap()
    vtok = nc.dram_tensor("vtok", [TOK, D], BF16, kind="ExternalOutput").ap()

    with ExitStack() as st:
        p = Prog(nc, st)
        sb = lambda name, shape, dt: st.enter_context(nc.sbuf_tensor(name, shape, dt))
        ps = lambda name, shape, dt: st.enter_context(nc.psum_tensor(name, shape, dt))

        x2s = [sb(f"x2s_{i}", [128, 512], F32) for i in range(NB)]
        ln16 = [sb(f"ln16_{i}", [128, TOK], BF16) for i in range(NB)]
        wcf = sb("wcf", [128, D], F32)
        wcb = [sb(f"wcb{i}", [128, D], BF16) for i in range(2)]
        evac = [sb(f"evac{i}", [128, 512], BF16) for i in range(2)]
        wvtile = [sb(f"wvt{i}", [128, 512], BF16) for i in range(NB)]
        wvf = sb("wvf", [128, 512], F32)
        ps_mm = [ps(f"ps_mm{i}", [128, 512], F32) for i in range(2)]
        ps_v = [ps(f"ps_v{i}", [128, 512], F32) for i in range(2)]
        lnsc = _ln_scratch(p, sb)

        for tt in range(4):
            sl = slice(tt * 512, tt * 512 + 512)
            for fb in range(NB):
                p.dma("sync", x2s[fb][:], x2T[fb * 128:(fb + 1) * 128, sl],
                      w=[x2s[fb]])
            emit_ln_fm(p, lnsc, ps_mm[0][0:1, :], ps_mm[0], ps_mm[1][0:1, :],
                       ps_mm[1], ps_v[0][:], ps_v[0], x2s,
                       [(ln16[fb][:, sl], (ln16[fb], tt)) for fb in range(NB)])
        lnkeys = [[(ln16[fb], tt) for tt in range(4)] for fb in range(NB)]

        def proj_fm(wsrc, outT):
            for ob in range(NB):
                wb = ob % 2
                for fb in range(NB):
                    p.dma("sync", wcf[:, fb * 128:(fb + 1) * 128],
                          wsrc[fb * 128:(fb + 1) * 128, ob * 128:(ob + 1) * 128],
                          w=[(wcf, fb)])
                p.op("vector", lambda e, wb=wb: e.tensor_copy(
                    out=wcb[wb][:], in_=wcf[:]),
                    r=[(wcf, f2) for f2 in range(NB)], w=[wcb[wb]])
                for tb in range(4):
                    pp = tb % 2
                    eb = tb % 2
                    for fb in range(NB):
                        p.op("tensor", lambda e, fb=fb, tb=tb, pp=pp, wb=wb: e.matmul(
                            out=ps_mm[pp][:], lhsT=wcb[wb][:, fb * 128:(fb + 1) * 128],
                            rhs=ln16[fb][:, tb * 512:(tb + 1) * 512],
                            start=(fb == 0), stop=(fb == NB - 1)),
                            r=[wcb[wb]] + lnkeys[fb], w=[ps_mm[pp]])
                    p.op("scalar", lambda e, pp=pp, eb=eb: e.activation(
                        out=evac[eb][:], in_=ps_mm[pp][:], func=AF.Copy),
                        r=[ps_mm[pp]], w=[evac[eb]])
                    p.dma("gpsimd",
                          outT[ob * 128:(ob + 1) * 128, tb * 512:(tb + 1) * 512],
                          evac[eb][:], r=[evac[eb]])

        proj_fm(wq, qT)
        proj_fm(wk, kT)

        for et in range(4):
            for fb in range(NB):
                p.dma("sync", wvf[:],
                      wv[fb * 128:(fb + 1) * 128, et * 512:(et + 1) * 512], w=[wvf])
                p.op("vector", lambda e, fb=fb: e.tensor_copy(
                    out=wvtile[fb][:], in_=wvf[:]), r=[wvf], w=[wvtile[fb]])
            for tb in range(NB):
                pp = tb % 2
                eb = tb % 2
                for fb in range(NB):
                    p.op("tensor", lambda e, fb=fb, tb=tb, pp=pp: e.matmul(
                        out=ps_v[pp][:], lhsT=ln16[fb][:, tb * 128:(tb + 1) * 128],
                        rhs=wvtile[fb][:], start=(fb == 0), stop=(fb == NB - 1)),
                        r=[wvtile[fb]] + lnkeys[fb], w=[ps_v[pp]])
                p.op("scalar", lambda e, pp=pp, eb=eb: e.activation(
                    out=evac[eb][:], in_=ps_v[pp][:], func=AF.Copy),
                    r=[ps_v[pp]], w=[evac[eb]])
                p.dma("gpsimd", vtok[tb * 128:(tb + 1) * 128, et * 512:(et + 1) * 512],
                      evac[eb][:], r=[evac[eb]])

        p.barrier_all_stores()
        p.emit()
    return nc


def build_attn1(nkc_past):
    nkc = 8 + nkc_past
    klen = nkc * 128
    nc = _bass()
    k5 = nc.dram_tensor("k5", [nkc, 128, NB, 128], BF16, kind="ExternalInput").ap()
    v5 = nc.dram_tensor("v5", [nkc, 128, NB, 128], BF16, kind="ExternalInput").ap()
    qTh = nc.dram_tensor("qTh", [D, 1024], BF16, kind="ExternalInput").ap()
    biask = nc.dram_tensor("biask", [128, nkc], F32, kind="ExternalInput").ap()
    maskm = nc.dram_tensor("maskm", [128, 1280], BF16, kind="ExternalInput").ap()
    oTh = nc.dram_tensor("oTh", [D, 1024], BF16, kind="ExternalOutput").ap()

    with ExitStack() as st:
        p = Prog(nc, st)
        sb = lambda name, shape, dt: st.enter_context(nc.sbuf_tensor(name, shape, dt))
        ps = lambda name, shape, dt: st.enter_context(nc.psum_tensor(name, shape, dt))

        q_s = [sb(f"q_s{i}", [128, 1024], BF16) for i in range(NB)]
        kbuf = [sb(f"kbuf{i}", [128, D], BF16) for i in range(2)]
        vbuf = [sb(f"vbuf{i}", [128, D], BF16) for i in range(2)]
        pt_all = [sb(f"pt{i}", [128, 512], BF16) for i in range(nkc)]
        mask_s = sb("mask_s", [128, 1280], BF16)
        bias_s = sb("bias_s", [128, nkc], F32)
        ones_b = sb("ones_b", [128, 1], BF16)
        ones1f = sb("ones1f", [1, 128], F32)
        l_sb = sb("l_sb", [1, 512], F32)
        rinv = sb("rinv", [1, 512], F32)
        rb_sb = sb("rb_sb", [128, 512], F32)
        o16 = [sb(f"o16{i}", [128, 512], BF16) for i in range(2)]
        sc_col = p.const_col(1.0 / np.sqrt(D))

        pool = [ps(f"pp{i}", [128, 512], F32) for i in range(8)]
        ps_s = [pool[6], pool[7]]
        ps_l = pool[4]
        ps_rb = pool[5]

        p.dma("sync", mask_s[:], maskm, w=[mask_s])
        p.dma("sync", bias_s[:], biask, w=[bias_s])
        p.op("vector", lambda e: e.memset(ones_b[:], 1.0), w=[ones_b])
        p.op("vector", lambda e: e.memset(ones1f[:], 1.0), w=[ones1f])
        for db in range(NB):
            p.dma("sync", q_s[db][:], qTh[db * 128:(db + 1) * 128, :], w=[q_s[db]])

        for qt in range(2):
            kclist = list(range(4 * qt + 4)) + list(range(8, nkc))
            qsl = slice(qt * 512, qt * 512 + 512)
            for ii, kc in enumerate(kclist):
                pp = ii % 2
                kb = ii % 2
                p.dma("sync", kbuf[kb][:],
                      k5[kc].rearrange("p db c -> p (db c)"), w=[kbuf[kb]])
                for db in range(NB):
                    p.op("tensor", lambda e, db=db, kb=kb, pp=pp, qsl=qsl: e.matmul(
                        out=ps_s[pp][:], lhsT=kbuf[kb][:, db * 128:(db + 1) * 128],
                        rhs=q_s[db][:, qsl], start=(db == 0), stop=(db == NB - 1)),
                        r=[kbuf[kb], q_s[db]], w=[ps_s[pp]])
                p.op("scalar", lambda e, pp=pp, kc=kc: e.activation(
                    out=pt_all[kc][:], in_=ps_s[pp][:], func=AF.Exp,
                    scale=sc_col[:], bias=bias_s[:, kc:kc + 1]),
                    r=[ps_s[pp], sc_col, bias_s], w=[pt_all[kc]])
                if 4 * qt <= kc < 4 * qt + 4:
                    dd = 384 - 128 * (kc - 4 * qt)
                    p.op("vector", lambda e, kc=kc, dd=dd: e.tensor_mul(
                        out=pt_all[kc][:], in0=pt_all[kc][:],
                        in1=mask_s[:, dd:dd + 512]),
                        r=[pt_all[kc], mask_s], w=[pt_all[kc]])
                p.op("tensor", lambda e, kc=kc, ii=ii, last=(ii == len(kclist) - 1): e.matmul(
                    out=ps_l[0:1, :], lhsT=ones_b[:], rhs=pt_all[kc][:],
                    start=(ii == 0), stop=last),
                    r=[ones_b, pt_all[kc]], w=[ps_l])
            p.op("vector", lambda e: e.tensor_copy(out=l_sb[:], in_=ps_l[0:1, :]),
                 r=[ps_l], w=[l_sb])
            p.op("vector", lambda e: e.reciprocal(out=rinv[:], in_=l_sb[:]),
                 r=[l_sb], w=[rinv])
            p.op("tensor", lambda e: e.matmul(
                out=ps_rb[:], lhsT=ones1f[:], rhs=rinv[:], start=True, stop=True),
                r=[ones1f, rinv], w=[ps_rb])
            p.op("vector", lambda e: e.tensor_copy(out=rb_sb[:], in_=ps_rb[:]),
                 r=[ps_rb], w=[rb_sb])
            for gb in range(2):
                for ii, kc in enumerate(kclist):
                    vb = ii % 2
                    p.dma("sync", vbuf[vb][:],
                          v5[kc].rearrange("p db c -> p (db c)"), w=[vbuf[vb]])
                    for dbl in range(8):
                        db = gb * 8 + dbl
                        p.op("tensor", lambda e, dbl=dbl, db=db, vb=vb, kc=kc,
                             ii=ii, last=(ii == len(kclist) - 1): e.matmul(
                            out=pool[dbl][:], lhsT=vbuf[vb][:, db * 128:(db + 1) * 128],
                            rhs=pt_all[kc][:], start=(ii == 0), stop=last),
                            r=[vbuf[vb], pt_all[kc]], w=[pool[dbl]])
                for dbl in range(8):
                    db = gb * 8 + dbl
                    ob = dbl % 2
                    p.op("vector", lambda e, dbl=dbl, ob=ob: e.tensor_mul(
                        out=o16[ob][:], in0=pool[dbl][:], in1=rb_sb[:]),
                        r=[pool[dbl], rb_sb], w=[o16[ob]])
                    p.dma("gpsimd", oTh[db * 128:(db + 1) * 128, qsl], o16[ob][:],
                          r=[o16[ob]])

        p.barrier_all_stores()
        p.emit()
    return nc


def build_resid_ln_part():
    nc = _bass()
    x2T = nc.dram_tensor("x2T", [D, TOK], F32, kind="ExternalInput").ap()
    oT = nc.dram_tensor("oT", [D, TOK], BF16, kind="ExternalInput").ap()
    x3T = nc.dram_tensor("x3T", [D, TOK], F32, kind="ExternalOutput").ap()
    hidT = nc.dram_tensor("hidT", [D, TOK], BF16, kind="ExternalOutput").ap()
    hpart = nc.dram_tensor("hpart", [128, NB], F32, kind="ExternalOutput").ap()

    with ExitStack() as st:
        p = Prog(nc, st)
        sb = lambda name, shape, dt: st.enter_context(nc.sbuf_tensor(name, shape, dt))
        ps = lambda name, shape, dt: st.enter_context(nc.psum_tensor(name, shape, dt))
        xs = [sb(f"xs_{i}", [128, 512], F32) for i in range(NB)]
        os_ = [sb(f"os_{i}", [128, 512], BF16) for i in range(NB)]
        lns = [sb(f"lns_{i}", [128, 512], BF16) for i in range(NB)]
        hacc = sb("hacc", [128, NB], F32)
        htmp = sb("htmp", [128, 1], F32)
        ps_m = ps("ps_m", [128, 512], F32)
        ps_s2 = ps("ps_s2", [128, 512], F32)
        ps_b = ps("ps_b", [128, 512], F32)
        lnsc = _ln_scratch(p, sb)
        p.op("vector", lambda e: e.memset(hacc[:], 0.0), w=[hacc])

        for tt in range(4):
            sl = slice(tt * 512, tt * 512 + 512)
            for fb in range(NB):
                p.dma("sync", xs[fb][:], x2T[fb * 128:(fb + 1) * 128, sl], w=[xs[fb]])
                p.dma("sync", os_[fb][:], oT[fb * 128:(fb + 1) * 128, sl], w=[os_[fb]])
                p.op("vector", lambda e, fb=fb: e.tensor_add(
                    out=xs[fb][:], in0=xs[fb][:], in1=os_[fb][:]),
                    r=[xs[fb], os_[fb]], w=[xs[fb]])
                p.dma("gpsimd", x3T[fb * 128:(fb + 1) * 128, sl], xs[fb][:],
                      r=[xs[fb]])
            emit_ln_fm(p, lnsc, ps_m[0:1, :], ps_m, ps_s2[0:1, :], ps_s2,
                       ps_b[:], ps_b, xs, lns)
            for fb in range(NB):
                p.dma("gpsimd", hidT[fb * 128:(fb + 1) * 128, sl], lns[fb][:],
                      r=[lns[fb]])
                p.op("vector", lambda e, fb=fb: e.reduce_sum(
                    out=htmp[:], in_=lns[fb][:], axis=AX.X),
                    r=[lns[fb]], w=[htmp])
                p.op("vector", lambda e, fb=fb: e.tensor_add(
                    out=hacc[:, fb:fb + 1], in0=hacc[:, fb:fb + 1], in1=htmp[:]),
                    r=[(hacc, fb), htmp], w=[(hacc, fb)])
        p.op("vector", lambda e: e.tensor_copy(out=hacc[:], in_=hacc[:]),
             r=[(hacc, fb) for fb in range(NB)], w=[hacc])
        p.dma("gpsimd", hpart, hacc[:], r=[hacc])
        p.barrier_all_stores()
        p.emit()
    return nc


def build_l1ffn():
    NEG = -1.0e30
    nc = _bass()
    x3T = nc.dram_tensor("x3T", [D, TOK], F32, kind="ExternalInput").ap()
    hidT = nc.dram_tensor("hidT", [D, TOK], BF16, kind="ExternalInput").ap()
    hp_a = nc.dram_tensor("hp_a", [128, NB], F32, kind="ExternalInput").ap()
    hp_b = nc.dram_tensor("hp_b", [128, NB], F32, kind="ExternalInput").ap()
    sim3 = nc.dram_tensor("sim3", [NB, 128, NE], F32, kind="ExternalInput").ap()
    gates = nc.dram_tensor("gates", [1, NE], F32, kind="ExternalInput").ap()
    wvh = nc.dram_tensor("wvh", [NE, EH, D], F32, kind="ExternalInput").ap()
    owh = nc.dram_tensor("owh", [NE, EH, D], F32, kind="ExternalInput").ap()
    identf = nc.dram_tensor("identf", [128, 128], F32, kind="ExternalInput").ap()
    identb = nc.dram_tensor("identb", [128, 128], BF16, kind="ExternalInput").ap()
    x4 = nc.dram_tensor("x4", [TOK, D], F32, kind="ExternalOutput").ap()

    with ExitStack() as st:
        p = Prog(nc, st)
        sb = lambda name, shape, dt: st.enter_context(nc.sbuf_tensor(name, shape, dt))
        ps = lambda name, shape, dt: st.enter_context(nc.psum_tensor(name, shape, dt))

        sruniv = sb("srun", [128, NB], F32)
        srsq = sb("srsq", [128, NB], F32)
        sim_s = [sb(f"sim_s{i}", [128, NE], F32) for i in range(NB)]
        simsq = sb("simsq", [128, NE], F32)
        ones_b = sb("ones_b", [128, 1], BF16)
        onesf = sb("onesf", [128, 1], F32)
        ones1f = sb("ones1f", [1, 128], F32)
        ones11 = sb("ones11", [1, 1], F32)
        id_f = sb("id_f", [128, 128], F32)
        id_b = sb("id_b", [128, 128], BF16)
        aff = sb("aff", [1, NE], F32)
        nsr = sb("nsr", [1, 1], F32)
        cn = sb("cn", [1, NE], F32)
        gat = sb("gat", [1, NE], F32)
        sigg = sb("sigg", [1, NE], F32)
        lgt = sb("lgt", [1, NE], F32)
        msk = sb("msk", [1, NE], F32)
        mmx = sb("mmx", [1, 1], F32)
        inact = sb("inact", [1, 1], F32)
        lb = sb("lb", [NE, NE], F32)
        lTc = sb("lTc", [NE, 1], F32)
        rk = sb("rk", [NE, NE], F32)
        rank = sb("rank", [NE, 1], F32)
        fbc = sb("fbc", [NE, 1], F32)
        fbr = sb("fbr", [1, NE], F32)
        negbig = sb("negbig", [1, NE], F32)
        sel = sb("sel", [1, NE], F32)
        mx = sb("mx", [1, 1], F32)
        nmx = sb("nmx", [1, 1], F32)
        sume = sb("sume", [1, 1], F32)
        rr = sb("rr", [1, NE], F32)
        rr16 = sb("rr16", [1, NE], BF16)
        rT16 = sb("rT16", [NE, 1], BF16)
        ones11b = sb("ones11b", [1, 1], BF16)
        gat_s = sb("gat_s", [1, NE], F32)

        wh_f = sb("wh_f", [NE, D], F32)
        wh_b = [sb(f"wh_b{i}", [NE, D], BF16) for i in range(2)]
        wvm = sb("wvm", [128, NB * EH], BF16)
        wodd = sb("wodd", [128, NB * EH], BF16)
        wodT = sb("wodT", [EH, NB * 128], BF16)
        hids = [sb(f"hids_{i}", [128, 512], BF16) for i in range(NB)]
        x3s = [sb(f"x3s_{i}", [128, 512], F32) for i in range(NB)]
        comb16 = sb("comb16", [EH, 512], BF16)
        x4t = sb("x4t", [128, D], F32)
        xev = [sb(f"xev{i}", [128, 128], F32) for i in range(2)]

        ps_mix = ps("ps_mix", [128, 1024], F32)
        ps_c = ps("ps_c", [EH, 512], F32)
        ps_f = [ps(f"ps_f{i}", [128, 512], F32) for i in range(2)]
        ps_t = [ps("ps_t0", [128, 128], F32)]
        ps_g = ps("ps_g", [NE, NE], F32)

        p.dma("sync", id_f[:], identf, w=[id_f])
        p.dma("sync", id_b[:], identb, w=[id_b])
        p.op("vector", lambda e: e.memset(ones_b[:], 1.0), w=[ones_b])
        p.op("vector", lambda e: e.memset(onesf[:], 1.0), w=[onesf])
        p.op("vector", lambda e: e.memset(ones1f[:], 1.0), w=[ones1f])
        p.op("vector", lambda e: e.memset(ones11[:], 1.0), w=[ones11])
        p.op("vector", lambda e: e.memset(ones11b[:], 1.0), w=[ones11b])
        p.op("vector", lambda e: e.memset(negbig[:], NEG), w=[negbig])

        p.dma("sync", sruniv[:], hp_a, w=[srunv_k := (sruniv, "a")])
        hp_bs = sb("hp_bs", [128, NB], F32)
        p.dma("sync", hp_bs[:], hp_b, w=[hp_bs])
        p.op("vector", lambda e: e.tensor_add(out=sruniv[:], in0=sruniv[:], in1=hp_bs[:]),
             r=[srunv_k, hp_bs], w=[sruniv])
        p.op("scalar", lambda e: e.activation(out=srsq[:], in_=sruniv[:], func=AF.Square),
             r=[sruniv], w=[srsq])
        for fb in range(NB):
            p.dma("sync", sim_s[fb][:], sim3[fb], w=[sim_s[fb]])
        for fb in range(NB):
            p.op("tensor", lambda e, fb=fb: e.matmul(
                out=ps_g[0:1, 0:NE], lhsT=srunv_sl(sruniv, fb), rhs=sim_s[fb][:],
                start=(fb == 0), stop=(fb == NB - 1)),
                r=[sruniv, sim_s[fb]], w=[ps_g])
        p.op("vector", lambda e: e.tensor_copy(out=aff[:], in_=ps_g[0:1, 0:NE]),
             r=[ps_g], w=[aff])
        for fb in range(NB):
            p.op("tensor", lambda e, fb=fb: e.matmul(
                out=ps_g[0:1, 0:1], lhsT=onesf[:], rhs=srsq[:, fb:fb + 1],
                start=(fb == 0), stop=(fb == NB - 1)),
                r=[onesf, srsq], w=[ps_g])
        p.op("vector", lambda e: e.tensor_copy(out=nsr[:], in_=ps_g[0:1, 0:1]),
             r=[ps_g], w=[nsr])
        for fb in range(NB):
            p.op("scalar", lambda e, fb=fb: e.activation(
                out=simsq[:], in_=sim_s[fb][:], func=AF.Square),
                r=[sim_s[fb]], w=[simsq])
            p.op("tensor", lambda e, fb=fb: e.matmul(
                out=ps_g[0:1, 0:NE], lhsT=onesf[:], rhs=simsq[:],
                start=(fb == 0), stop=(fb == NB - 1)),
                r=[onesf, simsq], w=[ps_g],
                extra_waits=())
        p.op("vector", lambda e: e.tensor_copy(out=cn[:], in_=ps_g[0:1, 0:NE]),
             r=[ps_g], w=[cn])
        p.op("vector", lambda e: e.tensor_scalar(
            out=cn[:], in0=cn[:], scalar1=nsr[0:1, 0:1], scalar2=None,
            op0=ALU.mult), r=[cn, nsr], w=[cn])
        p.op("scalar", lambda e: e.activation(out=cn[:], in_=cn[:], func=AF.Sqrt),
             r=[cn], w=[cn])
        p.op("vector", lambda e: e.reciprocal(out=cn[:], in_=cn[:]), r=[cn], w=[cn])
        p.op("vector", lambda e: e.tensor_mul(out=aff[:], in0=aff[:], in1=cn[:]),
             r=[aff, cn], w=[aff])
        p.dma("sync", sigg[:], gates, w=[sigg])
        p.op("scalar", lambda e: e.activation(out=sigg[:], in_=sigg[:], func=AF.Sigmoid),
             r=[sigg], w=[sigg])
        p.op("vector", lambda e: e.tensor_sub(out=lgt[:], in0=aff[:], in1=sigg[:]),
             r=[aff, sigg], w=[lgt])
        p.op("vector", lambda e: e.tensor_scalar_max(out=gat[:], in0=lgt[:], scalar1=0.0),
             r=[lgt], w=[gat])
        p.op("vector", lambda e: e.tensor_single_scalar(
            out=msk[:], in_=gat[:], scalar=0.0, op=ALU.is_gt), r=[gat], w=[msk])
        p.op("vector", lambda e: e.reduce_max(out=mmx[:], in_=msk[:], axis=AX.X),
             r=[msk], w=[mmx])
        p.op("vector", lambda e: e.tensor_scalar(
            out=inact[:], in0=mmx[:], scalar1=-1.0, scalar2=1.0,
            op0=ALU.mult, op1=ALU.add), r=[mmx], w=[inact])
        p.op("tensor", lambda e: e.matmul(
            out=ps_g[0:NE, 0:NE], lhsT=ones1f[0:1, 0:NE], rhs=lgt[:],
            start=True, stop=True), r=[ones1f, lgt], w=[ps_g])
        p.op("vector", lambda e: e.tensor_copy(out=lb[:], in_=ps_g[0:NE, 0:NE]),
             r=[ps_g], w=[lb])
        p.op("tensor", lambda e: e.matmul(
            out=ps_g[0:NE, 0:1], lhsT=lgt[:], rhs=ones11[:],
            start=True, stop=True), r=[lgt, ones11], w=[ps_g])
        p.op("vector", lambda e: e.tensor_copy(out=lTc[:], in_=ps_g[0:NE, 0:1]),
             r=[ps_g], w=[lTc])
        p.op("vector", lambda e: e.tensor_scalar(
            out=rk[:], in0=lb[:], scalar1=lTc[:], scalar2=None, op0=ALU.is_gt,
            op1=ALU.add, accum_out=rank[:]), r=[lb, lTc], w=[rk, rank])
        p.op("vector", lambda e: e.tensor_single_scalar(
            out=fbc[:], in_=rank[:], scalar=15.5, op=ALU.is_lt), r=[rank], w=[fbc])
        p.op("tensor", lambda e: e.matmul(
            out=ps_g[0:1, 0:NE], lhsT=fbc[:], rhs=id_f[0:NE, 0:NE],
            start=True, stop=True), r=[fbc, id_f], w=[ps_g])
        p.op("vector", lambda e: e.tensor_copy(out=fbr[:], in_=ps_g[0:1, 0:NE]),
             r=[ps_g], w=[fbr])
        p.op("vector", lambda e: e.tensor_scalar(
            out=fbr[:], in0=fbr[:], scalar1=inact[0:1, 0:1], scalar2=None,
            op0=ALU.mult), r=[fbr, inact], w=[fbr])
        p.op("vector", lambda e: e.tensor_max(out=msk[:], in0=msk[:], in1=fbr[:]),
             r=[msk, fbr], w=[msk])
        p.op("vector", lambda e: e.tensor_mul(out=sel[:], in0=gat[:], in1=msk[:]),
             r=[gat, msk], w=[sel])
        p.op("vector", lambda e: e.tensor_scalar(
            out=gat_s[:], in0=msk[:], scalar1=-NEG, scalar2=-NEG,
            op0=ALU.mult, op1=ALU.subtract), r=[msk], w=[gat_s])
        p.op("vector", lambda e: e.tensor_add(out=sel[:], in0=sel[:], in1=gat_s[:]),
             r=[sel, gat_s], w=[sel])
        p.op("vector", lambda e: e.reduce_max(out=mx[:], in_=sel[:], axis=AX.X),
             r=[sel], w=[mx])
        p.op("vector", lambda e: e.tensor_scalar(
            out=nmx[:], in0=mx[:], scalar1=-1.0, scalar2=None, op0=ALU.mult),
            r=[mx], w=[nmx])
        p.op("scalar", lambda e: e.activation(
            out=rr[:], in_=sel[:], func=AF.Exp, bias=nmx[0:1, 0:1],
            accum_out=sume[:]), r=[sel, nmx], w=[rr, sume])
        p.op("vector", lambda e: e.reciprocal(out=sume[:], in_=sume[:]),
             r=[sume], w=[sume])
        p.op("vector", lambda e: e.tensor_scalar(
            out=rr[:], in0=rr[:], scalar1=sume[0:1, 0:1], scalar2=None,
            op0=ALU.mult), r=[rr, sume], w=[rr])
        p.op("vector", lambda e: e.tensor_copy(out=rr16[:], in_=rr[:]),
             r=[rr], w=[rr16])
        p.op("tensor", lambda e: e.matmul(
            out=ps_g[0:NE, 0:1], lhsT=rr16[:], rhs=ones11b[:],
            start=True, stop=True), r=[rr16, ones11b], w=[ps_g])
        p.op("vector", lambda e: e.tensor_copy(out=rT16[:], in_=ps_g[0:NE, 0:1]),
             r=[ps_g], w=[rT16])

        def mix(wsrc, dst, dstname):
            for h in range(EH):
                hb = h % 2
                p.dma("sync", wh_f[:], wsrc[:, h, :], w=[wh_f])
                p.op("vector", lambda e, hb=hb: e.tensor_copy(
                    out=wh_b[hb][:], in_=wh_f[:]), r=[wh_f], w=[wh_b[hb]])
                for cb in range(NB):
                    p.op("tensor", lambda e, cb=cb, h=h, hb=hb: e.matmul(
                        out=ps_mix[:, cb * EH + h: cb * EH + h + 1],
                        lhsT=wh_b[hb][:, cb * 128:(cb + 1) * 128], rhs=rT16[:],
                        start=True, stop=True),
                        r=[wh_b[hb], rT16], w=[(ps_mix, (cb, h))])
            for cb in range(NB):
                p.op("vector", lambda e, cb=cb: e.tensor_copy(
                    out=dst[:, cb * EH:(cb + 1) * EH],
                    in_=ps_mix[:, cb * EH:(cb + 1) * EH]),
                    r=[(ps_mix, (cb, h2)) for h2 in range(EH)], w=[(dst, cb)])

        mix(wvh, wvm, "wvm")
        mix(owh, wodd, "wod")
        ps_tb = ps("ps_tb", [128, 128], BF16)
        for db in range(NB):
            p.op("tensor", lambda e, db=db: e.transpose(
                out=ps_tb[0:EH, :], in_=wodd[:, db * EH:(db + 1) * EH],
                identity=id_b[:]), r=[(wodd, db), id_b], w=[ps_tb])
            p.op("vector", lambda e, db=db: e.tensor_copy(
                out=wodT[:, db * 128:(db + 1) * 128], in_=ps_tb[0:EH, :]),
                r=[ps_tb], w=[(wodT, db)])

        for tt in range(4):
            sl = slice(tt * 512, tt * 512 + 512)
            for fb in range(NB):
                p.dma("sync", hids[fb][:], hidT[fb * 128:(fb + 1) * 128, sl],
                      w=[hids[fb]])
                p.dma("sync", x3s[fb][:], x3T[fb * 128:(fb + 1) * 128, sl],
                      w=[x3s[fb]])
            for cb in range(NB):
                p.op("tensor", lambda e, cb=cb: e.matmul(
                    out=ps_c[:], lhsT=wvm[:, cb * EH:(cb + 1) * EH],
                    rhs=hids[cb][:], start=(cb == 0), stop=(cb == NB - 1)),
                    r=[(wvm, cb), hids[cb]], w=[ps_c])
            p.op("scalar", lambda e: e.activation(
                out=comb16[:], in_=ps_c[:], func=AF.Copy), r=[ps_c], w=[comb16])
            for db in range(NB):
                pf = db % 2
                p.op("tensor", lambda e, db=db, pf=pf: e.matmul(
                    out=ps_f[pf][:], lhsT=wodT[:, db * 128:(db + 1) * 128],
                    rhs=comb16[:], start=True, stop=True),
                    r=[(wodT, db), comb16], w=[ps_f[pf]])
                p.op("vector", lambda e, db=db, pf=pf: e.tensor_add(
                    out=x3s[db][:], in0=x3s[db][:], in1=ps_f[pf][:]),
                    r=[x3s[db], ps_f[pf]], w=[x3s[db]])
            for tb4 in range(4):
                for db in range(NB):
                    p.op("tensor", lambda e, db=db, tb4=tb4: e.transpose(
                        out=ps_t[0][:], in_=x3s[db][:, tb4 * 128:(tb4 + 1) * 128],
                        identity=id_f[:]), r=[x3s[db], id_f], w=[ps_t[0]])
                    p.op("vector", lambda e, db=db: e.tensor_copy(
                        out=x4t[:, db * 128:(db + 1) * 128], in_=ps_t[0][:]),
                        r=[ps_t[0]], w=[(x4t, db)])
                p.dma("gpsimd", x4[tt * 512 + tb4 * 128: tt * 512 + (tb4 + 1) * 128, :],
                      x4t[:], r=[(x4t, db2) for db2 in range(NB)])

        p.barrier_all_stores()
        p.emit()
    return nc


def srunv_sl(srunv, fb):
    return srunv[:, fb:fb + 1]



_CACHE = {}


def _cached(key, fn):
    if key not in _CACHE:
        _CACHE[key] = fn()
    return _CACHE[key]


def _run(nc, in_maps):
    return run_bass_kernel_spmd(nc, in_maps, core_ids=list(range(NCORES))).results


def _c(a):
    return np.ascontiguousarray(a)


def kernel(**inputs):
    f32 = np.float32
    x = np.asarray(inputs["L0_attn_query"], f32)
    WqT0 = _c(np.asarray(inputs["L0_attn_Wq"], f32).T)
    WkT0 = _c(np.asarray(inputs["L0_attn_Wk"], f32).T)
    WvT0 = _c(np.asarray(inputs["L0_attn_Wv"], f32).T)
    WoT0 = _c(np.asarray(inputs["L0_attn_Wo"], f32).T)
    bq0 = np.asarray(inputs["L0_attn_bq"], f32)
    bk0 = np.asarray(inputs["L0_attn_bk"], f32)
    bv0 = np.asarray(inputs["L0_attn_bv"], f32)
    bo0 = np.asarray(inputs["L0_attn_bo"], f32)
    gup = np.asarray(inputs["L0_ffn_gate_up_packed"])
    gus = np.asarray(inputs["L0_ffn_gate_up_scales"])
    dnp = np.asarray(inputs["L0_ffn_down_packed"])
    dns = np.asarray(inputs["L0_ffn_down_scales"])
    Wq1 = _c(np.asarray(inputs["L1_attn_Wq"], f32))
    Wk1 = _c(np.asarray(inputs["L1_attn_Wk"], f32))
    Wv1 = _c(np.asarray(inputs["L1_attn_Wv"], f32))
    sim = np.asarray(inputs["L1_ffn_sim_matrix"], f32)
    gates = np.asarray(inputs["L1_ffn_gates"], f32)
    wvh = _c(np.asarray(inputs["L1_ffn_w_q"], f32)) if False else None
    wv1 = _c(np.asarray(inputs["L1_ffn_w_v"], f32))
    ow1 = _c(np.asarray(inputs["L1_ffn_o_weights"], f32))

    cosf, sinf = rope_tables_host()
    maskm = mask_master_host()
    identb = np.eye(128, dtype=ml_dtypes.bfloat16)
    identf = np.eye(128, dtype=np.float32)
    gupT = _c(gup.T)
    dnpT = _c(dnp.T)

    nc_dq = _cached("dequant", build_dequant)
    maps = []
    for c in range(NCORES):
        maps.append(dict(
            gup=_c(gupT[:, c * 1408:(c + 1) * 1408]),
            gus=_c(gus[:, c * 1408:(c + 1) * 1408]),
            dnp=_c(dnpT[:, c * 256:(c + 1) * 256]),
            dns=_c(dns[:, c * 256:(c + 1) * 256])))
    rdq = _run(nc_dq, maps)
    Wgu_full = np.concatenate([rdq[c]["wgu"] for c in range(NCORES)], axis=1)
    Wdn_full = np.concatenate([rdq[c]["wdn"] for c in range(NCORES)], axis=1)
    wgu5 = _c(Wgu_full.reshape(NB, 128, 88, 128).transpose(2, 1, 0, 3))
    wdn5 = _c(Wdn_full.reshape(44, 128, NB, 128).transpose(2, 1, 0, 3))

    nc_qkv0 = _cached("qkv0", build_qkv0)
    maps = []
    for c in range(NCORES):
        s, hh = c // 2, c % 2
        pos0 = hh * TOK
        maps.append(dict(
            x=_c(x[s, pos0:pos0 + TOK, :]),
            wq=WqT0, wk=WkT0, wv=WvT0, bq=bq0, bk=bk0, bv=bv0,
            cosf=_c(cosf[:, pos0:pos0 + TOK]), sinf=_c(sinf[:, pos0:pos0 + TOK]),
            identb=identb, identf=identf))
    r0 = _run(nc_qkv0, maps)

    nc_a0e = _cached("attn0e", lambda: build_attn0(0))
    maps = []
    for c in range(NCORES):
        s, g = c // 2, c % 2
        rows = slice(g * 1024, (g + 1) * 1024)
        maps.append(dict(
            qTh=_c(r0[2 * s]["qT"][rows, :]), kTh=_c(r0[2 * s]["kT"][rows, :]),
            vh=_c(r0[2 * s]["vtok"][:, rows]), maskm=maskm))
    rae = _run(nc_a0e, maps)
    nc_a0o = _cached("attn0o", lambda: build_attn0(TOK))
    maps = []
    for c in range(NCORES):
        s, g = c // 2, c % 2
        rows = slice(g * 1024, (g + 1) * 1024)
        maps.append(dict(
            qTh=_c(r0[2 * s + 1]["qT"][rows, :]),
            kTh=_c(np.concatenate([r0[2 * s]["kT"][rows, :],
                                   r0[2 * s + 1]["kT"][rows, :]], axis=1)),
            vh=_c(np.concatenate([r0[2 * s]["vtok"][:, rows],
                                  r0[2 * s + 1]["vtok"][:, rows]], axis=0)),
            maskm=maskm))
    rao = _run(nc_a0o, maps)

    def o_assemble(res_half, s):
        return np.concatenate([res_half[2 * s]["oTh"], res_half[2 * s + 1]["oTh"]],
                              axis=0)

    nc_ffn0 = _cached("ffn0", build_wo_ffn0)
    maps = []
    for c in range(NCORES):
        s, hh = c // 2, c % 2
        oT = o_assemble(rae if hh == 0 else rao, s)
        maps.append(dict(xT=_c(r0[c]["xT"]), oT=_c(oT), woT=WoT0, bo=bo0,
                         wgu5=wgu5, wdn5=wdn5))
    r2 = _run(nc_ffn0, maps)

    nc_qkv1 = _cached("qkv1", build_qkv1)
    maps = [dict(x2T=_c(r2[c]["x2T"]), wq=Wq1, wk=Wk1, wv=Wv1)
            for c in range(NCORES)]
    r3 = _run(nc_qkv1, maps)

    def attn1_maps(parity, nkc_past):
        nkc = 8 + nkc_past
        maps = []
        for c in range(NCORES):
            s, u = c // 2, c % 2
            crc = 2 * s + parity
            q0 = parity * TOK + u * 1024
            cols = slice(u * 1024, (u + 1) * 1024)
            kbuf = np.zeros((nkc * 128, D), ml_dtypes.bfloat16)
            vbuf = np.zeros((nkc * 128, D), ml_dtypes.bfloat16)
            kT_own = r3[crc]["kT"]; v_own = r3[crc]["vtok"]
            kbuf[:1024] = kT_own.T[cols]
            vbuf[:1024] = v_own[cols]
            past = []
            pastv = []
            if parity == 1:
                past.append(r3[2 * s]["kT"].T)
                pastv.append(r3[2 * s]["vtok"])
            if u == 1:
                past.append(kT_own.T[:1024])
                pastv.append(v_own[:1024])
            if past:
                pk = np.concatenate(past, axis=0)
                pv = np.concatenate(pastv, axis=0)
                kbuf[1024:1024 + pk.shape[0]] = pk
                vbuf[1024:1024 + pv.shape[0]] = pv
                npast_real = pk.shape[0] // 128
            else:
                npast_real = 0
            bias = np.full((128, nkc), -300.0 - SHIFT, np.float32)
            bias[:, :8] = -SHIFT
            bias[:, 8:8 + npast_real] = -SHIFT
            k5 = _c(np.asarray(kbuf).T.reshape(NB, 128, nkc, 128).transpose(2, 1, 0, 3))
            v5 = _c(np.asarray(vbuf).reshape(nkc, 128, NB, 128))
            maps.append(dict(k5=k5, v5=v5, qTh=_c(r3[crc]["qT"][:, cols]),
                             biask=bias, maskm=maskm))
        return maps

    nc_a1e = _cached("attn1e", lambda: build_attn1(8))
    ra1e = _run(nc_a1e, attn1_maps(0, 8))
    nc_a1o = _cached("attn1o", lambda: build_attn1(24))
    ra1o = _run(nc_a1o, attn1_maps(1, 24))

    def o1_assemble(res_half, s):
        return np.concatenate([res_half[2 * s]["oTh"], res_half[2 * s + 1]["oTh"]],
                              axis=1)

    nc_rp = _cached("residp", build_resid_ln_part)
    maps = []
    for c in range(NCORES):
        s, hh = c // 2, c % 2
        oT1 = o1_assemble(ra1e if hh == 0 else ra1o, s)
        maps.append(dict(x2T=_c(r2[c]["x2T"]), oT=_c(oT1)))
    r4 = _run(nc_rp, maps)

    nc_lf = _cached("l1ffn", build_l1ffn)
    maps = []
    sim3 = _c(sim.reshape(NB, 128, NE))
    g2 = _c(gates.reshape(1, NE))
    for c in range(NCORES):
        partner = c ^ 1
        maps.append(dict(x3T=_c(r4[c]["x3T"]), hidT=_c(r4[c]["hidT"]),
                         hp_a=_c(r4[c]["hpart"]), hp_b=_c(r4[partner]["hpart"]),
                         sim3=sim3, gates=g2, wvh=wv1, owh=ow1,
                         identf=identf, identb=identb))
    r5 = _run(nc_lf, maps)

    out = np.empty((BATCH, SEQ, D), np.float32)
    for c in range(NCORES):
        s, hh = c // 2, c % 2
        out[s, hh * TOK:(hh + 1) * TOK, :] = r5[c]["x4"]
    return out
